# version 18
# baseline (speedup 1.0000x reference)
import numpy as np
import concourse.bass as bass
import concourse.mybir as mybir
from concourse.bass_utils import run_bass_kernel_spmd

F32 = mybir.dt.float32
BF16 = mybir.dt.bfloat16
AF = mybir.ActivationFunctionType
ALU = mybir.AluOpType
AX = mybir.AxisListType

B, L, C = 2, 8192, 2048
H, G, D, W = 16, 2, 128, 256
NCORE = 8
RT = 4
TO = 512
TQ = 384
NC_ = 16
NI = 64
SCL = float(1.0 / np.sqrt(128.0))
EPS = 1e-5

ENGS = ("sync", "gpsimd", "tensor", "vector", "scalar")


class Gen:

    def __init__(self, sems):
        self.prog = {e: [] for e in ENGS}
        self.sems = sems
        self.cnt = {k: 0 for k in sems}
        self.mark = {e: {k: 0 for k in sems} for e in ENGS}
        self.bank_free = {}

    def wait(self, eng, sem, val):
        if val is None or val <= 0 or val <= self.mark[eng][sem]:
            return
        self.mark[eng][sem] = val
        h = self.sems[sem]
        self.prog[eng].append(lambda e, h=h, v=val: e.wait_ge(h, v))

    def op(self, eng, f, inc=None, n=1):
        if inc is None:
            self.prog[eng].append(f)
            return None
        self.cnt[inc] += n
        h = self.sems[inc]
        self.prog[eng].append(lambda e, f=f, h=h, m=n: f(e).then_inc(h, m))
        return self.cnt[inc]

    def use_bank(self, bank):
        for sem, val in self.bank_free.get(bank, ()):
            self.wait("tensor", sem, val)

    def consume_bank(self, bank, sem, val):
        self.bank_free[bank] = [(sem, val)]

    def consume_bank_add(self, bank, sem, val):
        self.bank_free.setdefault(bank, []).append((sem, val))


def _emit_ln(g, aps, nch, T, src_sq, src_sub, src_bf, dst_bf, sq_blk,
             bsum, bsq, bM, bR, src_bf_ready):
    ones_bf = aps["ones_bf"]
    ones_bf_ready = aps["ones_bf_ready"]
    mean_row, msq_row, tmp_row = (
        aps["mean_row"], aps["msq_row"], aps["tmp_row"])
    tmp_ln = aps["tmp_ln"]

    (nb_sum, ap_sum), (nb_sq, ap_sq) = bsum, bsq
    (nb_M, ap_M), (nb_R, ap_R) = bM, bR
    sum_ps = ap_sum[0:1, 0:T]
    sq_ps = ap_sq[0:1, 0:T]
    M_ps = ap_M[:, 0:T]
    R_ps = ap_R[:, 0:T]

    g.wait("tensor", *ones_bf_ready)
    g.use_bank(nb_sum)
    for cc in range(nch):
        sem, val = src_bf_ready(cc)
        g.wait("tensor", sem, val)
        xb = src_bf(cc)
        g.op("tensor", lambda e, o=sum_ps, l=ones_bf[:, 0:1], r=xb,
             s=(cc == 0), t=(cc == nch - 1): e.matmul(o, l, r, start=s, stop=t),
             inc="pe" if cc == nch - 1 else None)
    sum_stop = g.cnt["pe"]

    sq_done = [None] * nch
    sumsq_done = [None] * nch

    def emit_square(cc):
        ap, sem, val = src_sq(cc, ("sc", sq_done[cc - 2]) if cc >= 2 else None)
        g.wait("scalar", sem, val)
        if cc >= 2:
            g.wait("scalar", "pe", sumsq_done[cc - 2])
        sq_done[cc] = g.op("scalar",
                           lambda e, o=sq_blk(cc % 2), i=ap:
                           e.activation(out=o, in_=i, func=AF.Square), inc="sc")

    g.use_bank(nb_sq)
    emit_square(0)
    emit_square(1)
    for cc in range(nch):
        g.wait("tensor", "sc", sq_done[cc])
        sumsq_done[cc] = g.op("tensor",
                              lambda e, o=sq_ps, l=ones_bf[:, 0:1], r=sq_blk(cc % 2),
                              s=(cc == 0), t=(cc == nch - 1):
                              e.matmul(o, l, r, start=s, stop=t), inc="pe")
        if cc + 2 < nch:
            emit_square(cc + 2)
    sumsq_stop = g.cnt["pe"]

    g.wait("vector", "pe", sum_stop)
    v_mean = g.op("vector", lambda e, o=mean_row[0:1, 0:T], i=sum_ps:
                  e.tensor_scalar(out=o, in0=i, scalar1=1.0 / C, scalar2=None,
                                  op0=ALU.mult), inc="v")
    g.wait("vector", "pe", sumsq_stop)
    v_msq = g.op("vector", lambda e, o=msq_row[0:1, 0:T], i=sq_ps:
                 e.tensor_scalar(out=o, in0=i, scalar1=1.0 / C, scalar2=None,
                                 op0=ALU.mult), inc="v")
    g.consume_bank(nb_sum, "v", v_mean)
    g.consume_bank(nb_sq, "v", v_msq)
    g.wait("vector", "v", v_mean)
    mean_bf, rstd_bf = aps["mean_bf"], aps["rstd_bf"]
    v_meanbf = g.op("vector", lambda e, o=mean_bf[0:1, 0:T], i=mean_row[0:1, 0:T]:
                    e.tensor_copy(out=o, in_=i), inc="v")
    v_m2 = g.op("vector", lambda e, o=tmp_row[0:1, 0:T], a=mean_row[0:1, 0:T]:
                e.tensor_tensor(out=o, in0=a, in1=a, op=ALU.mult), inc="v")
    g.wait("vector", "v", v_m2)
    v_var = g.op("vector", lambda e, o=msq_row[0:1, 0:T], a=msq_row[0:1, 0:T],
                 b=tmp_row[0:1, 0:T]:
                 e.scalar_tensor_tensor(out=o, in0=a, scalar=EPS, in1=b,
                                        op0=ALU.add, op1=ALU.subtract), inc="v")
    g.wait("scalar", "v", v_var)
    sc_ln = g.op("scalar", lambda e, o=tmp_row[0:1, 0:T], i=msq_row[0:1, 0:T]:
                 e.activation(out=o, in_=i, func=AF.Ln), inc="sc")
    g.wait("scalar", "sc", sc_ln)
    sc_rstd = g.op("scalar", lambda e, o=rstd_bf[0:1, 0:T], i=tmp_row[0:1, 0:T]:
                   e.activation(out=o, in_=i, func=AF.Exp, scale=-0.5), inc="sc")

    g.use_bank(nb_M)
    g.wait("tensor", "v", v_meanbf)
    g.op("tensor", lambda e, o=M_ps, l=ones_bf[0:1, 0:128], r=mean_bf[0:1, 0:T]:
         e.matmul(o, l, r, start=True, stop=True), inc="pe")
    g.use_bank(nb_R)
    g.wait("tensor", "sc", sc_rstd)
    pe_bcast = g.op("tensor", lambda e, o=R_ps, l=ones_bf[0:1, 0:128],
                    r=rstd_bf[0:1, 0:T]:
                    e.matmul(o, l, r, start=True, stop=True), inc="pe")

    g.wait("vector", "pe", pe_bcast)
    sub_done = [None] * nch
    mul_done = [None] * nch

    def emit_sub(cc):
        if cc >= 2:
            g.wait("vector", "v", mul_done[cc - 2])
        ap, sem, val = src_sub(cc, ("v", sub_done[cc - 2]) if cc >= 2 else None)
        g.wait("vector", sem, val)
        sub_done[cc] = g.op("vector", lambda e, o=tmp_ln(cc % 2)[:, 0:T], a=ap,
                            b=M_ps: e.tensor_tensor(out=o, in0=a, in1=b,
                                                    op=ALU.subtract), inc="v")

    def emit_mul(cc):
        g.wait("vector", "v", sub_done[cc])
        dst = dst_bf(cc)
        mul_done[cc] = g.op("vector", lambda e, o=dst, a=tmp_ln(cc % 2)[:, 0:T],
                            b=R_ps: e.tensor_tensor(out=o, in0=a, in1=b,
                                                    op=ALU.mult), inc="v")

    emit_sub(0)
    for cc in range(1, nch):
        emit_sub(cc)
        emit_mul(cc - 1)
    emit_mul(nch - 1)
    g.consume_bank(nb_M, "v", mul_done[nch - 1])
    g.consume_bank(nb_R, "v", mul_done[nch - 1])
    return mul_done, sumsq_done


def build_nc():
    nc = bass.Bass("TRN2", target_bir_lowering=False, debug=False)

    d_xq = nc.dram_tensor("xq", [C, 1152], F32, kind="ExternalInput").ap()
    d_xo = nc.dram_tensor("xo", [C, 2048], F32, kind="ExternalInput").ap()
    d_wq = nc.dram_tensor("wq", [16, 128, 2048], F32, kind="ExternalInput").ap()
    d_wk = nc.dram_tensor("wk", [2, 128, 2048], F32, kind="ExternalInput").ap()
    d_wv = nc.dram_tensor("wv", [128, 4096], F32, kind="ExternalInput").ap()
    d_wo = nc.dram_tensor("wo", [16, 128, 2048], F32, kind="ExternalInput").ap()
    d_wgu = nc.dram_tensor("wgu", [128, 128, 2048], F32, kind="ExternalInput").ap()
    d_wd = nc.dram_tensor("wd", [16, 128, 8192], F32, kind="ExternalInput").ap()
    d_bq = nc.dram_tensor("bq", [128, 16], F32, kind="ExternalInput").ap()
    d_bk = nc.dram_tensor("bk", [128, 2], F32, kind="ExternalInput").ap()
    d_bo = nc.dram_tensor("bo", [128, 16], F32, kind="ExternalInput").ap()
    d_bv = nc.dram_tensor("bv", [1, 256], F32, kind="ExternalInput").ap()
    d_ones = nc.dram_tensor("ones", [128, 128], F32, kind="ExternalInput").ap()
    d_eye = nc.dram_tensor("eye", [128, 128], F32, kind="ExternalInput").ap()
    d_y = nc.dram_tensor("y", [C, 2048], F32, kind="ExternalOutput").ap()
    s_wq = nc.dram_tensor("s_wq", [16, 128, 2048], BF16, kind="Internal").ap()
    s_wo = nc.dram_tensor("s_wo", [16, 128, 2048], BF16, kind="Internal").ap()
    s_wgu = nc.dram_tensor("s_wgu", [128, 128, 2048], BF16, kind="Internal").ap()
    s_wd = nc.dram_tensor("s_wd", [16, 128, 8192], BF16, kind="Internal").ap()

    import contextlib
    stack = contextlib.ExitStack()
    sb = lambda name, shape, dt: stack.enter_context(nc.sbuf_tensor(name, shape, dt))

    xq_bf = sb("xq_bf", [128, NC_ * TQ], BF16)
    xo_sb = sb("xo_sb", [128, NC_ * TO], F32)
    ln2_bf = sb("ln2_bf", [128, NC_ * TO], BF16)
    pwA = sb("pwA", [128, 8192], BF16)
    pwB = sb("pwB", [128, 8192], BF16)
    pwC = sb("pwC", [128, 2048], BF16)
    bq_sb = sb("bq_sb", [128, 16], F32)
    bk_sb = sb("bk_sb", [128, 2], F32)
    bo_sb = sb("bo_sb", [128, 16], F32)
    bqs_sb = sb("bqs_sb", [128, 16], F32)
    bv_bf = sb("bv_bf", [1, 256], BF16)
    ones_bf = sb("ones_bf", [128, 128], BF16)
    eye_bf = sb("eye_bf", [128, 128], BF16)
    silu0 = sb("silu0", [128, 512], F32)
    silu1 = sb("silu1", [128, 512], F32)
    x2_0 = sb("x2_0", [128, 512], F32)
    x2_1 = sb("x2_1", [128, 512], F32)
    qr_sb = sb("qr_sb", [128, H * 4 * 128], BF16)
    kr_sb = sb("kr_sb", [128, G * 5 * 128], BF16)
    vr_sb = sb("vr_sb", [128, 4 * 256], BF16)
    xqf_sb = sb("xqf_sb", [128, NC_ * 256], F32)

    pf = [stack.enter_context(nc.psum_tensor(f"pf{i}", [128, 512], F32))
          for i in range(6)]
    pb = [stack.enter_context(nc.psum_tensor(f"pb{i}", [128, 512], BF16))
          for i in range(2)]
    pfa = [p[:] for p in pf]
    pba = [p[:] for p in pb]

    sems = {}
    for k in ("pe", "v", "sc", "w", "x", "st", "cw"):
        sems[k] = stack.enter_context(nc.semaphore(f"s_{k}"))
    g = Gen(sems)

    silu_b = lambda j: (silu0 if j == 0 else silu1)[:]
    x2_b = lambda j: (x2_0 if j == 0 else x2_1)[:]
    def quarter(j):
        buf = pwA if j < 4 else (pwB if j < 8 else pwC)
        return buf[:, (j % 4) * 2048:(j % 4 + 1) * 2048]

    aps = dict(ones_bf=ones_bf[:], ones_bf_ready=None)

    q_last_read = [0] * 10
    q_cw = [0] * 10
    CACHE_RT = {"wq": 99, "wo": 99, "wgu": 99, "wd": 99}
    cache_wr = {k: {} for k in CACHE_RT}
    CONV_RT = {"wq": 0, "wo": 1, "wgu": 99, "wd": 99}
    conv_done = {k: {} for k in CONV_RT}
    conv_avail = {k: {} for k in CONV_RT}
    SQ_CACHE = {"wgu": (lambda i: i % 2 if i < 64 else (i + 1) % 2),
                "wd": (lambda i: i % 2)}

    def gq_dma(out, in_, pe_war=0):
        g.wait("gpsimd", "pe", pe_war)
        return g.op("gpsimd", lambda e, o=out, i=in_: e.dma_start(out=o, in_=i),
                    inc="w", n=16)

    pending_cw = []

    def _emit_cw(ent):
        bf16_slab, dst, val, stream, idx, quarters = ent
        g.wait("gpsimd", "pe", max(q_last_read[j] for j in quarters))
        cwv = g.op("gpsimd", lambda e, o=bf16_slab, i=dst:
                   e.dma_start(out=o, in_=i), inc="cw", n=16)
        cache_wr[stream][idx] = cwv
        for j in quarters:
            q_cw[j] = cwv

    def flush_cw(quarters=None, keep=2):
        while pending_cw and (len(pending_cw) > keep or (
                quarters is not None
                and any(set(e[5]) & set(quarters) for e in pending_cw))):
            _emit_cw(pending_cw.pop(0))

    def slab_load(stream, idx, rt, dst, fp32_slab, bf16_slab, quarters):
        flush_cw(quarters)
        g.wait("gpsimd", "pe", max(q_last_read[j] for j in quarters))
        g.wait("gpsimd", "cw", max(q_cw[j] for j in quarters))
        wrt = SQ_CACHE[stream](idx) if stream in SQ_CACHE else None
        if (conv_done.get(stream, {}).get(idx) is not None
                and rt >= conv_avail[stream].get(idx, 99)):
            g.wait("gpsimd", "w", conv_done[stream][idx])
            val = g.op("gpsimd", lambda e, o=dst, i=bf16_slab:
                       e.dma_start(out=o, in_=i), inc="w", n=16)
        elif wrt is not None and rt > wrt:
            g.wait("gpsimd", "cw", cache_wr[stream][idx])
            val = g.op("gpsimd", lambda e, o=dst, i=bf16_slab:
                       e.dma_start(out=o, in_=i), inc="w", n=16)
        else:
            val = g.op("gpsimd", lambda e, o=dst, i=fp32_slab:
                       e.dma_start(out=o, in_=i), inc="w", n=16)
            if wrt is not None and rt == wrt:
                pending_cw.append((bf16_slab, dst, val, stream, idx, quarters))
                flush_cw()
        return val

    def sq_dma(out, in_):
        return g.op("sync", lambda e, o=out, i=in_: e.dma_start(out=o, in_=i),
                    inc="x", n=16)

    x_bq = sq_dma(bq_sb[:], d_bq)
    x_bk = sq_dma(bk_sb[:], d_bk)
    x_bo = sq_dma(bo_sb[:], d_bo)
    w_onesb = gq_dma(ones_bf[:], d_ones)
    w_eye = gq_dma(eye_bf[:], d_eye)
    w_bv = gq_dma(bv_bf[0:1, :], d_bv)
    aps["ones_bf_ready"] = ("w", w_onesb)
    g.wait("vector", "x", x_bq)
    v_bqs = g.op("vector", lambda e: e.tensor_scalar(
        out=bqs_sb[:], in0=bq_sb[:], scalar1=SCL, scalar2=None, op0=ALU.mult),
        inc="v")

    prev_down_mm_stop = 0
    prev_v_stop = 0
    prev_devac = 0
    store_cnt = 0

    def new_chunks(rt):
        return [0, 1, 2] if rt == 0 else [2 * rt + 1, 2 * rt + 2]

    def emit_xqbf_load(rt, pe_gate=0):
        ms = new_chunks(rt)
        c0, T = 128 * ms[0], 128 * len(ms)
        in3 = d_xq[:, c0:c0 + T].rearrange("(c p) t -> p c t", p=128)
        out3b = xq_bf[:].rearrange("p (c t) -> p c t", t=TQ)[:, :, 0:T]
        g.wait("gpsimd", "pe", pe_gate)
        return g.op("gpsimd", lambda e, o=out3b, i=in3:
                    e.dma_start(out=o, in_=i), inc="w", n=16)

    def ln1_passes(rt):
        if rt == 0:
            return [([0, 1], 0, 0), ([2], 256, 128)]
        return [(new_chunks(rt), 0, 0)]

    def emit_xqf_load(pms, v_gate=0):
        ms, off, sh = pms
        c0, T = 128 * ms[0], 128 * len(ms)
        in3 = d_xq[:, c0:c0 + T].rearrange("(c p) t -> p c t", p=128)
        out3 = xqf_sb[:].rearrange("p (c t) -> p c t", t=256)[:, :, sh:sh + T]
        g.wait("gpsimd", "v", v_gate)
        return g.op("gpsimd", lambda e, o=out3, i=in3:
                    e.dma_start(out=o, in_=i), inc="w", n=16)

    def emit_ln1(pms, w_xqbf, w_xqf, down_war):
        ms, off, sh = pms
        T = 128 * len(ms)
        srcf = lambda cc, war: (xqf_sb[:, cc * 256 + sh: cc * 256 + sh + T],
                                "w", w_xqf)
        blk_bf = lambda cc: xq_bf[:, cc * TQ + off: cc * TQ + off + T]
        sq_blk1 = lambda j: sq_sb[:, j * 512:j * 512 + T]
        g.wait("scalar", "pe", down_war)
        mul1, _ = _emit_ln(
            g, aps, NC_, T, srcf, srcf, blk_bf, blk_bf, sq_blk1,
            ("pf0", pfa[0]), ("pf1", pfa[1]), ("pf2", pfa[2]), ("pf3", pfa[3]),
            src_bf_ready=lambda cc: ("w", w_xqbf))
        return mul1

    q_ready, k_ready, v_ready = {}, {}, {}

    pre_xqbf = emit_xqbf_load(0)
    pre_xqf = [emit_xqf_load(ln1_passes(0)[0])]

    for rt in range(RT):
        ascope = contextlib.ExitStack()
        asb = lambda name, shape, dt: ascope.enter_context(
            nc.sbuf_tensor(f"{name}_{rt}", shape, dt))
        attn_sb = asb("attn", [128, NC_ * TO], BF16)
        wv_sb = asb("wv_sb", [128, 4096], BF16)
        mean_row = asb("mean_row", [1, 512], F32)
        msq_row = asb("msq_row", [1, 512], F32)
        tmp_row = asb("tmp_row", [1, 512], F32)
        mean_bf = asb("mean_bf", [1, 512], BF16)
        rstd_bf = asb("rstd_bf", [1, 512], BF16)
        tmp_ln0 = asb("tmp_ln0", [128, 512], F32)
        tmp_ln1 = asb("tmp_ln1", [128, 512], F32)
        sq_sb = asb("sq_sb", [128, 1024], BF16)
        nm_sb = asb("nm_sb", [128, 8], F32)
        rs_sb = asb("rs_sb", [128, 8], F32)
        ri_sb = asb("ri_sb", [128, 8], F32)
        tmp_ln = lambda j: (tmp_ln0 if j == 0 else tmp_ln1)[:]
        aps.update(mean_row=mean_row[:], msq_row=msq_row[:], tmp_row=tmp_row[:],
                   mean_bf=mean_bf[:], rstd_bf=rstd_bf[:], tmp_ln=tmp_ln)
        p_sb0 = asb("p0", [128, 512], BF16)
        p_sb1 = asb("p1", [128, 512], BF16)
        p_sb2 = asb("p2", [128, 512], BF16)
        p_sb3 = asb("p3", [128, 512], BF16)
        pt_sb0 = asb("pt0", [128, 512], BF16)
        pt_sb1 = asb("pt1", [128, 512], BF16)
        p_sbs = [p_sb0, p_sb1, p_sb2, p_sb3]
        p_sb = lambda j: p_sbs[j][:]
        pt_sb = lambda j: (pt_sb0 if j == 0 else pt_sb1)[:]

        ms = new_chunks(rt)
        Tn = 128 * len(ms)

        in3o = d_xo[:, rt * TO:(rt + 1) * TO].rearrange("(c p) t -> p c t", p=128)
        out3o = xo_sb[:].rearrange("p (c t) -> p c t", t=TO)
        g.wait("sync", "v", prev_devac)
        x_xo = sq_dma(out3o, in3o)
        wq_pre = {}
        for fc in range(2):
            wq_pre[fc] = slab_load("wq", fc, rt, quarter(fc % 4), d_wq[fc],
                                   s_wq[fc], [fc % 4])
        wk_q = {}
        for gix in range(2):
            g.wait("gpsimd", "pe", q_last_read[4 + gix])
            wk_q[gix] = g.op("gpsimd", lambda e, o=quarter(4 + gix), i=d_wk[gix]:
                             e.dma_start(out=o, in_=i), inc="w", n=16)
        w_wv = gq_dma(wv_sb[:], d_wv, pe_war=prev_down_mm_stop)

        for pi, pms in enumerate(ln1_passes(rt)):
            if pi >= len(pre_xqf):
                pre_xqf.append(emit_xqf_load(pms, v_gate=mul1[NC_ - 1]))
            mul1 = emit_ln1(pms, pre_xqbf, pre_xqf[pi],
                            down_war=prev_down_mm_stop)

        blkn_bf = lambda hc: xq_bf[:, hc * TQ: hc * TQ + Tn]
        for fc in range(NC_):
            qd = fc % 4
            if fc in wq_pre:
                w_slab = wq_pre[fc]
            else:
                w_slab = slab_load("wq", fc, rt, quarter(qd), d_wq[fc], s_wq[fc], [qd])
            bank = f"pf{fc % 2}"
            psq = pfa[fc % 2][:, 0:Tn]
            g.use_bank(bank)
            g.wait("tensor", "w", w_slab)
            for hc in range(NC_):
                g.wait("tensor", "v", mul1[hc])
                g.op("tensor", lambda e, o=psq, l=quarter(qd)[:, hc * 128:(hc + 1) * 128],
                     r=blkn_bf(hc), s=(hc == 0), t=(hc == NC_ - 1):
                     e.matmul(o, l, r, start=s, stop=t),
                     inc="pe" if hc == NC_ - 1 else None)
            mm_stop = g.cnt["pe"]
            q_last_read[qd] = mm_stop
            g.wait("scalar", "pe", mm_stop)
            g.wait("scalar", "v", v_bqs)
            ev = None
            for j, m in enumerate(ms):
                ev = g.op("scalar", lambda e,
                          o=qr_sb[:, fc * 512 + (m % 4) * 128: fc * 512 + (m % 4) * 128 + 128],
                          i=pfa[fc % 2][:, j * 128:(j + 1) * 128],
                          bias=bqs_sb[:, fc:fc + 1]:
                          e.activation(out=o, in_=i, func=AF.Identity,
                                       bias=bias, scale=SCL),
                          inc="sc" if j == len(ms) - 1 else None)
            g.consume_bank(bank, "sc", ev)
        for m in ms:
            q_ready[m] = g.cnt["sc"]
        if rt == CONV_RT["wq"]:
            for fc in range(NC_):
                conv_done["wq"][fc] = g.op(
                    "gpsimd", lambda e, o=s_wq[fc], i=d_wq[fc]:
                    e.dma_start(out=o, in_=i), inc="w", n=16)
                conv_avail["wq"][fc] = rt + 1
        if rt == CONV_RT["wo"]:
            for fc in range(NC_):
                conv_done["wo"][fc] = g.op(
                    "gpsimd", lambda e, o=s_wo[fc], i=d_wo[fc]:
                    e.dma_start(out=o, in_=i), inc="w", n=16)
                conv_avail["wo"][fc] = rt + 1

        for gix in range(2):
            bank = f"pf{gix % 2}"
            psk = pfa[gix % 2][:, 0:Tn]
            g.use_bank(bank)
            g.wait("tensor", "w", wk_q[gix])
            for hc in range(NC_):
                g.op("tensor", lambda e, o=psk,
                     l=quarter(4 + gix)[:, hc * 128:(hc + 1) * 128],
                     r=blkn_bf(hc), s=(hc == 0), t=(hc == NC_ - 1):
                     e.matmul(o, l, r, start=s, stop=t),
                     inc="pe" if hc == NC_ - 1 else None)
            mm_stop = g.cnt["pe"]
            q_last_read[4 + gix] = mm_stop
            g.wait("scalar", "pe", mm_stop)
            g.wait("scalar", "x", x_bk)
            ev = None
            for j, m in enumerate(ms):
                slots = [m % 4] + ([4] if m % 4 == 0 else [])
                for s4 in slots:
                    ev = g.op("scalar", lambda e,
                              o=kr_sb[:, gix * 640 + s4 * 128: gix * 640 + s4 * 128 + 128],
                              i=pfa[gix % 2][:, j * 128:(j + 1) * 128],
                              bias=bk_sb[:, gix:gix + 1]:
                              e.activation(out=o, in_=i, func=AF.Identity,
                                           bias=bias),
                              inc="sc" if (j == len(ms) - 1 and s4 == slots[-1]) else None)
            g.consume_bank(bank, "sc", ev)
        for m in ms:
            k_ready[m] = g.cnt["sc"]
        g.wait("tensor", "w", w_wv)
        g.wait("tensor", "w", w_bv)
        for j in range(len(ms)):
            g.use_bank(f"pf{2 + j}")
        for hc in range(NC_):
            for j in range(len(ms)):
                psv = pfa[2 + j][:, 0:256]
                g.op("tensor", lambda e, o=psv,
                     l=xq_bf[:, hc * TQ + j * 128: hc * TQ + (j + 1) * 128],
                     r=wv_sb[:, hc * 256:(hc + 1) * 256], s=(hc == 0), t=False:
                     e.matmul(o, l, r, start=s, stop=t))
        v_stop = [None] * len(ms)
        for j in range(len(ms)):
            psv = pfa[2 + j][:, 0:256]
            v_stop[j] = g.op("tensor", lambda e, o=psv, l=ones_bf[0:1, 0:128],
                             r=bv_bf[0:1, 0:256]:
                             e.matmul(o, l, r, start=False, stop=True), inc="pe")
        prev_v_stop = g.cnt["pe"]
        for j, m in enumerate(ms):
            g.wait("vector", "pe", v_stop[j])
            v_ready[m] = g.op("vector", lambda e,
                              o=vr_sb[:, (m % 4) * 256:(m % 4) * 256 + 256],
                              i=pfa[2 + j][:, 0:256]: e.tensor_copy(out=o, in_=i),
                              inc="v")
            g.consume_bank(f"pf{2 + j}", "v", v_ready[m])

        SBANKS = [0, 1, 4, 5]
        exp_done = [None] * 32
        scale_done = [None] * 32
        ptev_done = [None] * 32
        oev_done = [None] * 32
        s_stop = [None] * 32
        pt_stop = [None] * 32
        o_stop = [None] * 32

        def at_S(wh):
            wl, h = wh // 16, wh % 16
            w = 2 * rt + wl
            gg = h % 2
            bi = SBANKS[wh % 4]
            g.use_bank(f"pf{bi}")
            g.wait("tensor", "sc", q_ready[w + 1])
            g.wait("tensor", "sc", k_ready[w + 1])
            for qc in range(2):
                sl = (w + qc) % 4
                s_stop[wh] = g.op(
                    "tensor", lambda e, o=pfa[bi][:, qc * 256:(qc + 1) * 256],
                    l=qr_sb[:, h * 512 + sl * 128: h * 512 + sl * 128 + 128],
                    r=kr_sb[:, gg * 640 + (w % 4) * 128: gg * 640 + (w % 4) * 128 + 256]:
                    e.matmul(o, l, r, start=True, stop=True),
                    inc="pe" if qc == 1 else None)

        def at_max(wh):
            bi = SBANKS[wh % 4]
            g.wait("vector", "pe", s_stop[wh])
            if wh >= 4:
                g.wait("vector", "sc", exp_done[wh - 4])
            sp3 = pfa[bi].rearrange("p (a k) -> p a k", k=256)
            g.op("vector", lambda e, o=nm_sb[:, (wh % 4) * 2: (wh % 4) * 2 + 2],
                 i=sp3: e.reduce_max(out=o, in_=i, axis=AX.X, negate=True),
                 inc="v")

        nm_done = [None] * 32

        def at_exp(wh):
            bi = SBANKS[wh % 4]
            g.wait("scalar", "v", nm_done[wh])
            if wh >= 4:
                g.wait("scalar", "pe", pt_stop[wh - 4])
            for qc in range(2):
                exp_done[wh] = g.op(
                    "scalar", lambda e, o=p_sb(wh % 4)[:, qc * 256:(qc + 1) * 256],
                    i=pfa[bi][:, qc * 256:(qc + 1) * 256],
                    b=nm_sb[:, (wh % 4) * 2 + qc: (wh % 4) * 2 + qc + 1]:
                    e.activation(out=o, in_=i, func=AF.Exp, bias=b),
                    inc="sc" if qc == 1 else None)
            g.consume_bank(f"pf{bi}", "sc", exp_done[wh])

        def at_norm(wh):
            g.wait("vector", "sc", exp_done[wh])
            p3 = p_sb(wh % 4).rearrange("p (a k) -> p a k", k=256)
            sum_done = g.op("vector",
                            lambda e, o=rs_sb[:, (wh % 4) * 2: (wh % 4) * 2 + 2],
                            i=p3: e.reduce_sum(out=o, in_=i, axis=AX.X), inc="v")
            g.wait("vector", "v", sum_done)
            ri_done = g.op("vector",
                           lambda e, o=ri_sb[:, (wh % 4) * 2: (wh % 4) * 2 + 2],
                           i=rs_sb[:, (wh % 4) * 2: (wh % 4) * 2 + 2]:
                           e.reciprocal(out=o, in_=i), inc="v")
            g.wait("vector", "v", ri_done)
            for qc in range(2):
                scale_done[wh] = g.op(
                    "vector", lambda e, o=p_sb(wh % 4)[:, qc * 256:(qc + 1) * 256],
                    s1=ri_sb[:, (wh % 4) * 2 + qc: (wh % 4) * 2 + qc + 1]:
                    e.tensor_scalar(out=o, in0=o, scalar1=s1, scalar2=None,
                                    op0=ALU.mult), inc="v" if qc == 1 else None)

        def at_PT(wh):
            g.use_bank(f"pb{wh % 2}")
            g.wait("tensor", "v", scale_done[wh])
            for qc in range(2):
                for kc in range(2):
                    pt_stop[wh] = g.op(
                        "tensor", lambda e,
                        o=pba[wh % 2][:, kc * 256 + qc * 128: kc * 256 + (qc + 1) * 128],
                        i=p_sb(wh % 4)[:, qc * 256 + kc * 128: qc * 256 + (kc + 1) * 128]:
                        e.transpose(o, i, eye_bf[:]),
                        inc="pe" if (qc == 1 and kc == 1) else None)

        def at_ptevac(wh):
            g.wait("scalar", "pe", pt_stop[wh])
            if wh >= 2:
                g.wait("scalar", "pe", o_stop[wh - 2])
            ptev_done[wh] = g.op("scalar", lambda e, o=pt_sb(wh % 2), i=pba[wh % 2]:
                                 e.activation(out=o, in_=i, func=AF.Identity),
                                 inc="sc")
            g.consume_bank(f"pb{wh % 2}", "sc", ptev_done[wh])

        def at_O(wh):
            wl, gg = wh // 16, (wh % 16) % 2
            w = 2 * rt + wl
            g.use_bank(f"pf{2 + wh % 2}")
            g.wait("tensor", "sc", ptev_done[wh])
            g.wait("tensor", "v", v_ready[w + 1])
            for kc in range(2):
                sl = (w + kc) % 4
                o_stop[wh] = g.op(
                    "tensor", lambda e, o=pfa[2 + wh % 2][:, 0:256],
                    l=vr_sb[:, sl * 256 + gg * 128: sl * 256 + (gg + 1) * 128],
                    r=pt_sb(wh % 2)[:, kc * 256:(kc + 1) * 256], s=(kc == 0), t=(kc == 1):
                    e.matmul(o, l, r, start=s, stop=t),
                    inc="pe" if kc == 1 else None)

        def at_oevac(wh):
            wl, h = wh // 16, wh % 16
            g.wait("vector", "pe", o_stop[wh])
            oev_done[wh] = g.op(
                "vector", lambda e, o=attn_sb[:, h * TO + wl * 256: h * TO + wl * 256 + 256],
                i=pfa[2 + wh % 2][:, 0:256]: e.tensor_copy(out=o, in_=i), inc="v")
            g.consume_bank(f"pf{2 + wh % 2}", "v", oev_done[wh])

        for i in range(32 + 8):
            if i < 32:
                at_S(i)
            if 0 <= i - 1 < 32:
                at_max(i - 1)
                nm_done[i - 1] = g.cnt["v"]
            if 0 <= i - 2 < 32:
                at_exp(i - 2)
            if 0 <= i - 3 < 32:
                at_norm(i - 3)
            if 0 <= i - 4 < 32:
                at_PT(i - 4)
            if 0 <= i - 5 < 32:
                at_ptevac(i - 5)
            if 0 <= i - 6 < 32:
                at_O(i - 6)
            if 0 <= i - 7 < 32:
                at_oevac(i - 7)
        attn_ready = oev_done[31]

        stt_done = [None] * NC_
        copy_done = [None] * NC_
        for fc in range(NC_):
            qd = fc % 4
            w_slab = slab_load("wo", fc, rt, quarter(qd), d_wo[fc], s_wo[fc], [qd])
            bank = f"pf{fc % 2}"
            psx = pfa[fc % 2][:, 0:TO]
            g.use_bank(bank)
            g.wait("tensor", "w", w_slab)
            g.wait("tensor", "v", attn_ready)
            for hc in range(NC_):
                g.op("tensor", lambda e, o=psx,
                     l=quarter(qd)[:, hc * 128:(hc + 1) * 128],
                     r=attn_sb[:, hc * TO:(hc + 1) * TO], s=(hc == 0),
                     t=(hc == NC_ - 1): e.matmul(o, l, r, start=s, stop=t),
                     inc="pe" if hc == NC_ - 1 else None)
            mm_stop = g.cnt["pe"]
            q_last_read[qd] = mm_stop
            g.wait("vector", "pe", mm_stop)
            g.wait("vector", "x", x_xo)
            g.wait("vector", "x", x_bo)
            stt_done[fc] = g.op(
                "vector", lambda e, o=xo_sb[:, fc * TO:(fc + 1) * TO], i=psx,
                s1=bo_sb[:, fc:fc + 1]:
                e.scalar_tensor_tensor(out=o, in0=i, scalar=s1, in1=o,
                                       op0=ALU.add, op1=ALU.add), inc="v")
            g.consume_bank(bank, "v", stt_done[fc])
            if fc >= 1:
                g.wait("vector", "v", stt_done[fc - 1])
                copy_done[fc - 1] = g.op(
                    "vector", lambda e, o=ln2_bf[:, (fc - 1) * TO: fc * TO],
                    i=xo_sb[:, (fc - 1) * TO: fc * TO]:
                    e.tensor_copy(out=o, in_=i), inc="v")
        g.wait("vector", "v", stt_done[NC_ - 1])
        copy_done[NC_ - 1] = g.op(
            "vector", lambda e, o=ln2_bf[:, (NC_ - 1) * TO: NC_ * TO],
            i=xo_sb[:, (NC_ - 1) * TO: NC_ * TO]: e.tensor_copy(out=o, in_=i),
            inc="v")

        blko_f = lambda cc: xo_sb[:, cc * TO:(cc + 1) * TO]
        blk2_bf = lambda cc: ln2_bf[:, cc * TO:(cc + 1) * TO]
        sq_blk2 = lambda j: sq_sb[:, j * 512:(j + 1) * 512]
        src2 = lambda cc, war: (blko_f(cc), "v", stt_done[cc])
        mul2, _ = _emit_ln(
            g, aps, NC_, TO, src2, src2, blk2_bf, blk2_bf, sq_blk2,
            ("pf4", pfa[4]), ("pf5", pfa[5]), ("pf2", pfa[2]), ("pf3", pfa[3]),
            src_bf_ready=lambda cc: ("v", copy_done[cc]))

        ascope.close()
        fscope = contextlib.ExitStack()
        act_sb = fscope.enter_context(
            nc.sbuf_tensor(f"act_{rt}", [128, NI * TO], BF16))

        silu_done = [None] * NI
        act_done = [None] * NI
        w_wd0 = None
        for ic in range(NI):
            if ic == 2:
                w_wd0 = slab_load("wd", 0, rt, pwA[:], d_wd[0], s_wd[0],
                                  [0, 1, 2, 3])
            qd_g = 4 + (2 * ic) % 5
            w_g = slab_load("wgu", ic, rt, quarter(qd_g), d_wgu[ic],
                            s_wgu[ic], [qd_g])
            qd_u = 4 + (2 * ic + 1) % 5
            w_u = slab_load("wgu", 64 + ic, rt, quarter(qd_u), d_wgu[64 + ic],
                            s_wgu[64 + ic], [qd_u])
            gb = [0, 1, 4][ic % 3]
            ub = [2, 3, 5][ic % 3]
            gbank, ubank = f"pf{gb}", f"pf{ub}"
            psg = pfa[gb][:, 0:TO]
            psu = pfa[ub][:, 0:TO]
            g.use_bank(gbank)
            g.wait("tensor", "w", w_g)
            for hc in range(NC_):
                g.wait("tensor", "v", mul2[hc])
                g.op("tensor", lambda e, o=psg,
                     l=quarter(qd_g)[:, hc * 128:(hc + 1) * 128],
                     r=blk2_bf(hc), s=(hc == 0), t=(hc == NC_ - 1):
                     e.matmul(o, l, r, start=s, stop=t),
                     inc="pe" if hc == NC_ - 1 else None)
            g_stop = g.cnt["pe"]
            q_last_read[qd_g] = g_stop
            g.use_bank(ubank)
            g.wait("tensor", "w", w_u)
            for hc in range(NC_):
                g.op("tensor", lambda e, o=psu,
                     l=quarter(qd_u)[:, hc * 128:(hc + 1) * 128],
                     r=blk2_bf(hc), s=(hc == 0), t=(hc == NC_ - 1):
                     e.matmul(o, l, r, start=s, stop=t),
                     inc="pe" if hc == NC_ - 1 else None)
            u_stop = g.cnt["pe"]
            q_last_read[qd_u] = u_stop
            g.wait("scalar", "pe", g_stop)
            if ic >= 2:
                g.wait("scalar", "v", act_done[ic - 2])
            silu_done[ic] = g.op("scalar", lambda e, o=silu_b(ic % 2), i=psg:
                                 e.activation(out=o, in_=i, func=AF.Silu),
                                 inc="sc")
            g.consume_bank(gbank, "sc", silu_done[ic])
            g.wait("vector", "sc", silu_done[ic])
            g.wait("vector", "pe", u_stop)
            act_done[ic] = g.op(
                "vector", lambda e, o=act_sb[:, ic * TO:(ic + 1) * TO],
                a=silu_b(ic % 2), b=psu:
                e.tensor_tensor(out=o, in0=a, in1=b, op=ALU.mult), inc="v")
            g.consume_bank(ubank, "v", act_done[ic])

        if rt + 1 < RT:
            pre_xqbf = emit_xqbf_load(rt + 1, pe_gate=prev_v_stop)
            pre_xqf = [emit_xqf_load(p, v_gate=mul1[NC_ - 1])
                       for p in ln1_passes(rt + 1)]

        for cc in range(NC_):
            if cc % 2 == 0:
                wd_buf = pwA
                quarters = [0, 1, 2, 3]
            else:
                wd_buf = pwB
                quarters = [4, 5, 6, 7]
            if cc == 0:
                w_wd = w_wd0
            else:
                w_wd = slab_load("wd", cc, rt, wd_buf[:], d_wd[cc], s_wd[cc],
                                 quarters)
            bank = f"pf{4 + cc % 2}"
            psd = pfa[4 + cc % 2][:, 0:TO]
            g.use_bank(bank)
            g.wait("tensor", "w", w_wd)
            for ic in range(NI):
                g.wait("tensor", "v", act_done[ic])
                g.op("tensor", lambda e, o=psd,
                     l=wd_buf[:, ic * 128:(ic + 1) * 128],
                     r=act_sb[:, ic * TO:(ic + 1) * TO], s=(ic == 0),
                     t=(ic == NI - 1): e.matmul(o, l, r, start=s, stop=t),
                     inc="pe" if ic == NI - 1 else None)
            mm_stop = g.cnt["pe"]
            if cc % 2 == 0:
                for j in range(4):
                    q_last_read[j] = mm_stop
            else:
                for j in range(4, 8):
                    q_last_read[j] = mm_stop
            g.wait("vector", "pe", mm_stop)
            g.wait("vector", "st", 16 * max(0, store_cnt - 1))
            devac = g.op("vector", lambda e, o=x2_b(cc % 2), i=psd,
                         b=xo_sb[:, cc * TO:(cc + 1) * TO]:
                         e.tensor_tensor(out=o, in0=i, in1=b, op=ALU.add),
                         inc="v")
            g.consume_bank(bank, "v", devac)
            g.wait("sync", "v", devac)
            g.op("sync", lambda e, o=d_y[cc * 128:(cc + 1) * 128,
                                         rt * TO:(rt + 1) * TO], i=x2_b(cc % 2):
                 e.dma_start(out=o, in_=i), inc="st", n=16)
            store_cnt += 1
            prev_devac = devac
        flush_cw(keep=0)
        prev_down_mm_stop = g.cnt["pe"]
        fscope.close()

    g.op("sync", lambda e, v=16 * store_cnt: e.wait_ge(sems["st"], v))

    with nc.Block() as block:
        @block.sync
        def _(sync):
            for f in g.prog["sync"]:
                f(sync)

        @block.gpsimd
        def _(gpsimd):
            for f in g.prog["gpsimd"]:
                f(gpsimd)

        @block.tensor
        def _(tensor):
            for f in g.prog["tensor"]:
                f(tensor)

        @block.vector
        def _(vector):
            for f in g.prog["vector"]:
                f(vector)

        @block.scalar
        def _(scalar):
            for f in g.prog["scalar"]:
                f(scalar)

    stack.close()
    return nc


_cache = {}


def kernel(L0_attn_x, L0_attn_Wq, L0_attn_bq, L0_attn_Wk, L0_attn_bk,
           L0_attn_Wv, L0_attn_bv, L0_attn_Wo, L0_attn_bo,
           L0_ffn_w_gate_up, L0_ffn_w_down):
    x = np.asarray(L0_attn_x, dtype=np.float32)
    Wq = np.asarray(L0_attn_Wq, dtype=np.float32)
    Wk = np.asarray(L0_attn_Wk, dtype=np.float32)
    Wv = np.asarray(L0_attn_Wv, dtype=np.float32)
    Wo = np.asarray(L0_attn_Wo, dtype=np.float32)
    Wgu = np.asarray(L0_ffn_w_gate_up, dtype=np.float32)
    Wd = np.asarray(L0_ffn_w_down, dtype=np.float32)

    wq_blk = np.ascontiguousarray(Wq.reshape(16, 128, 16, 128).transpose(2, 1, 0, 3))
    wk_blk = np.ascontiguousarray(Wk.reshape(16, 128, 2, 128).transpose(2, 1, 0, 3))
    wo_blk = np.ascontiguousarray(Wo.reshape(16, 128, 16, 128).transpose(2, 1, 0, 3))
    wv_blk = np.ascontiguousarray(
        Wv.reshape(16, 128, 256).transpose(1, 0, 2)).reshape(128, 4096)
    wgu_blk = np.ascontiguousarray(
        Wgu.reshape(128, 128, 16, 128).transpose(0, 3, 2, 1))
    wd_blk = np.ascontiguousarray(
        Wd.reshape(16, 128, 64, 128).transpose(0, 3, 2, 1))
    bq2 = np.ascontiguousarray(np.asarray(L0_attn_bq, np.float32).reshape(16, 128).T)
    bk2 = np.ascontiguousarray(np.asarray(L0_attn_bk, np.float32).reshape(2, 128).T)
    bo2 = np.ascontiguousarray(np.asarray(L0_attn_bo, np.float32).reshape(16, 128).T)
    bv2 = np.ascontiguousarray(np.asarray(L0_attn_bv, np.float32).reshape(1, 256))
    ones = np.ones((128, 128), np.float32)
    eye = np.eye(128, dtype=np.float32)

    if "nc" not in _cache:
        _cache["nc"] = build_nc()
    nc = _cache["nc"]

    in_maps = []
    for c in range(NCORE):
        b, a = c // 4, c % 4
        xqT = np.ascontiguousarray(x[b, 1024 * a:1024 * a + 1152, :].T)
        xoT = np.ascontiguousarray(x[b, 2048 * a:2048 * (a + 1), :].T)
        in_maps.append(dict(
            xq=xqT, xo=xoT, wq=wq_blk, wk=wk_blk, wv=wv_blk, wo=wo_blk,
            wgu=wgu_blk, wd=wd_blk, bq=bq2, bk=bk2, bo=bo2, bv=bv2,
            ones=ones, eye=eye))

    res = run_bass_kernel_spmd(nc, in_maps, core_ids=list(range(NCORE)))

    out = np.empty((B, L, C), np.float32)
    for c in range(NCORE):
        b, a = c // 4, c % 4
        out[b, 2048 * a:2048 * (a + 1), :] = np.asarray(res.results[c]["y"]).T
    return out


# version 19
# speedup vs baseline: 1.1770x; 1.1770x over previous
import numpy as np
import concourse.bass as bass
import concourse.mybir as mybir
from concourse.bass_utils import run_bass_kernel_spmd

F32 = mybir.dt.float32
BF16 = mybir.dt.bfloat16
AF = mybir.ActivationFunctionType
ALU = mybir.AluOpType
AX = mybir.AxisListType

B, L, C = 2, 8192, 2048
H, G, D, W = 16, 2, 128, 256
NCORE = 8
RT = 4
TO = 512
TQ = 384
NC_ = 16
NI = 64
SCL = float(1.0 / np.sqrt(128.0))
EPS = 1e-5

ENGS = ("sync", "gpsimd", "tensor", "vector", "scalar")


class Gen:

    def __init__(self, sems):
        self.prog = {e: [] for e in ENGS}
        self.sems = sems
        self.cnt = {k: 0 for k in sems}
        self.mark = {e: {k: 0 for k in sems} for e in ENGS}
        self.bank_free = {}

    def wait(self, eng, sem, val):
        if val is None or val <= 0 or val <= self.mark[eng][sem]:
            return
        self.mark[eng][sem] = val
        h = self.sems[sem]
        self.prog[eng].append(lambda e, h=h, v=val: e.wait_ge(h, v))

    def op(self, eng, f, inc=None, n=1):
        if inc is None:
            self.prog[eng].append(f)
            return None
        self.cnt[inc] += n
        h = self.sems[inc]
        self.prog[eng].append(lambda e, f=f, h=h, m=n: f(e).then_inc(h, m))
        return self.cnt[inc]

    def use_bank(self, bank):
        for sem, val in self.bank_free.get(bank, ()):
            self.wait("tensor", sem, val)

    def consume_bank(self, bank, sem, val):
        self.bank_free[bank] = [(sem, val)]

    def consume_bank_add(self, bank, sem, val):
        self.bank_free.setdefault(bank, []).append((sem, val))


def _emit_ln(g, aps, nch, T, src_sq, src_sub, src_bf, dst_bf, sq_blk,
             bsum, bsq, bM, bR, src_bf_ready):
    ones_bf = aps["ones_bf"]
    ones_bf_ready = aps["ones_bf_ready"]
    mean_row, msq_row, tmp_row = (
        aps["mean_row"], aps["msq_row"], aps["tmp_row"])
    tmp_ln = aps["tmp_ln"]

    (nb_sum, ap_sum), (nb_sq, ap_sq) = bsum, bsq
    (nb_M, ap_M), (nb_R, ap_R) = bM, bR
    sum_ps = ap_sum[0:1, 0:T]
    sq_ps = ap_sq[0:1, 0:T]
    M_ps = ap_M[:, 0:T]
    R_ps = ap_R[:, 0:T]

    g.wait("tensor", *ones_bf_ready)
    g.use_bank(nb_sum)
    for cc in range(nch):
        sem, val = src_bf_ready(cc)
        g.wait("tensor", sem, val)
        xb = src_bf(cc)
        g.op("tensor", lambda e, o=sum_ps, l=ones_bf[:, 0:1], r=xb,
             s=(cc == 0), t=(cc == nch - 1): e.matmul(o, l, r, start=s, stop=t),
             inc="pe" if cc == nch - 1 else None)
    sum_stop = g.cnt["pe"]

    sq_done = [None] * nch
    sumsq_done = [None] * nch

    def emit_square(cc):
        ap, sem, val = src_sq(cc, ("sc", sq_done[cc - 2]) if cc >= 2 else None)
        g.wait("scalar", sem, val)
        if cc >= 2:
            g.wait("scalar", "pe", sumsq_done[cc - 2])
        sq_done[cc] = g.op("scalar",
                           lambda e, o=sq_blk(cc % 2), i=ap:
                           e.activation(out=o, in_=i, func=AF.Square), inc="sc")

    g.use_bank(nb_sq)
    emit_square(0)
    emit_square(1)
    for cc in range(nch):
        g.wait("tensor", "sc", sq_done[cc])
        sumsq_done[cc] = g.op("tensor",
                              lambda e, o=sq_ps, l=ones_bf[:, 0:1], r=sq_blk(cc % 2),
                              s=(cc == 0), t=(cc == nch - 1):
                              e.matmul(o, l, r, start=s, stop=t), inc="pe")
        if cc + 2 < nch:
            emit_square(cc + 2)
    sumsq_stop = g.cnt["pe"]

    g.wait("vector", "pe", sum_stop)
    v_mean = g.op("vector", lambda e, o=mean_row[0:1, 0:T], i=sum_ps:
                  e.tensor_scalar(out=o, in0=i, scalar1=1.0 / C, scalar2=None,
                                  op0=ALU.mult), inc="v")
    g.wait("vector", "pe", sumsq_stop)
    v_msq = g.op("vector", lambda e, o=msq_row[0:1, 0:T], i=sq_ps:
                 e.tensor_scalar(out=o, in0=i, scalar1=1.0 / C, scalar2=None,
                                 op0=ALU.mult), inc="v")
    g.consume_bank(nb_sum, "v", v_mean)
    g.consume_bank(nb_sq, "v", v_msq)
    g.wait("vector", "v", v_mean)
    mean_bf, rstd_bf = aps["mean_bf"], aps["rstd_bf"]
    v_meanbf = g.op("vector", lambda e, o=mean_bf[0:1, 0:T], i=mean_row[0:1, 0:T]:
                    e.tensor_copy(out=o, in_=i), inc="v")
    v_m2 = g.op("vector", lambda e, o=tmp_row[0:1, 0:T], a=mean_row[0:1, 0:T]:
                e.tensor_tensor(out=o, in0=a, in1=a, op=ALU.mult), inc="v")
    g.wait("vector", "v", v_m2)
    v_var = g.op("vector", lambda e, o=msq_row[0:1, 0:T], a=msq_row[0:1, 0:T],
                 b=tmp_row[0:1, 0:T]:
                 e.scalar_tensor_tensor(out=o, in0=a, scalar=EPS, in1=b,
                                        op0=ALU.add, op1=ALU.subtract), inc="v")
    g.wait("scalar", "v", v_var)
    sc_ln = g.op("scalar", lambda e, o=tmp_row[0:1, 0:T], i=msq_row[0:1, 0:T]:
                 e.activation(out=o, in_=i, func=AF.Ln), inc="sc")
    g.wait("scalar", "sc", sc_ln)
    sc_rstd = g.op("scalar", lambda e, o=rstd_bf[0:1, 0:T], i=tmp_row[0:1, 0:T]:
                   e.activation(out=o, in_=i, func=AF.Exp, scale=-0.5), inc="sc")

    g.use_bank(nb_M)
    g.wait("tensor", "v", v_meanbf)
    g.op("tensor", lambda e, o=M_ps, l=ones_bf[0:1, 0:128], r=mean_bf[0:1, 0:T]:
         e.matmul(o, l, r, start=True, stop=True), inc="pe")
    g.use_bank(nb_R)
    g.wait("tensor", "sc", sc_rstd)
    pe_bcast = g.op("tensor", lambda e, o=R_ps, l=ones_bf[0:1, 0:128],
                    r=rstd_bf[0:1, 0:T]:
                    e.matmul(o, l, r, start=True, stop=True), inc="pe")

    g.wait("vector", "pe", pe_bcast)
    sub_done = [None] * nch
    mul_done = [None] * nch

    def emit_sub(cc):
        if cc >= 2:
            g.wait("vector", "v", mul_done[cc - 2])
        ap, sem, val = src_sub(cc, ("v", sub_done[cc - 2]) if cc >= 2 else None)
        g.wait("vector", sem, val)
        sub_done[cc] = g.op("vector", lambda e, o=tmp_ln(cc % 2)[:, 0:T], a=ap,
                            b=M_ps: e.tensor_tensor(out=o, in0=a, in1=b,
                                                    op=ALU.subtract), inc="v")

    def emit_mul(cc):
        g.wait("vector", "v", sub_done[cc])
        dst = dst_bf(cc)
        mul_done[cc] = g.op("vector", lambda e, o=dst, a=tmp_ln(cc % 2)[:, 0:T],
                            b=R_ps: e.tensor_tensor(out=o, in0=a, in1=b,
                                                    op=ALU.mult), inc="v")

    emit_sub(0)
    for cc in range(1, nch):
        emit_sub(cc)
        emit_mul(cc - 1)
    emit_mul(nch - 1)
    g.consume_bank(nb_M, "v", mul_done[nch - 1])
    g.consume_bank(nb_R, "v", mul_done[nch - 1])
    return mul_done, sumsq_done


def build_nc():
    nc = bass.Bass("TRN2", target_bir_lowering=False, debug=False)

    d_xq = nc.dram_tensor("xq", [C, 1152], F32, kind="ExternalInput").ap()
    d_xo = nc.dram_tensor("xo", [C, 2048], F32, kind="ExternalInput").ap()
    d_wq = nc.dram_tensor("wq", [16, 128, 2048], F32, kind="ExternalInput").ap()
    d_wk = nc.dram_tensor("wk", [2, 128, 2048], F32, kind="ExternalInput").ap()
    d_wv = nc.dram_tensor("wv", [128, 4096], F32, kind="ExternalInput").ap()
    d_wo = nc.dram_tensor("wo", [16, 128, 2048], F32, kind="ExternalInput").ap()
    d_wgu = nc.dram_tensor("wgu", [128, 128, 2048], F32, kind="ExternalInput").ap()
    d_wd = nc.dram_tensor("wd", [16, 128, 8192], F32, kind="ExternalInput").ap()
    d_bq = nc.dram_tensor("bq", [128, 16], F32, kind="ExternalInput").ap()
    d_bk = nc.dram_tensor("bk", [128, 2], F32, kind="ExternalInput").ap()
    d_bo = nc.dram_tensor("bo", [128, 16], F32, kind="ExternalInput").ap()
    d_bv = nc.dram_tensor("bv", [1, 256], F32, kind="ExternalInput").ap()
    d_ones = nc.dram_tensor("ones", [128, 128], F32, kind="ExternalInput").ap()
    d_eye = nc.dram_tensor("eye", [128, 128], F32, kind="ExternalInput").ap()
    d_y = nc.dram_tensor("y", [C, 2048], F32, kind="ExternalOutput").ap()
    s_wq = nc.dram_tensor("s_wq", [16, 128, 2048], BF16, kind="Internal").ap()
    s_wo = nc.dram_tensor("s_wo", [16, 128, 2048], BF16, kind="Internal").ap()
    s_wgu = nc.dram_tensor("s_wgu", [128, 128, 2048], BF16, kind="Internal").ap()
    s_wd = nc.dram_tensor("s_wd", [16, 128, 8192], BF16, kind="Internal").ap()

    import contextlib
    stack = contextlib.ExitStack()
    sb = lambda name, shape, dt: stack.enter_context(nc.sbuf_tensor(name, shape, dt))

    xq_bf = sb("xq_bf", [128, NC_ * TQ], BF16)
    xo_sb = sb("xo_sb", [128, NC_ * TO], F32)
    ln2_bf = sb("ln2_bf", [128, NC_ * TO], BF16)
    pwA = sb("pwA", [128, 8192], BF16)
    pwB = sb("pwB", [128, 8192], BF16)
    pwC = sb("pwC", [128, 2048], BF16)
    bq_sb = sb("bq_sb", [128, 16], F32)
    bk_sb = sb("bk_sb", [128, 2], F32)
    bo_sb = sb("bo_sb", [128, 16], F32)
    bqs_sb = sb("bqs_sb", [128, 16], F32)
    bv_bf = sb("bv_bf", [1, 256], BF16)
    ones_bf = sb("ones_bf", [128, 128], BF16)
    eye_bf = sb("eye_bf", [128, 128], BF16)
    silu0 = sb("silu0", [128, 512], F32)
    silu1 = sb("silu1", [128, 512], F32)
    x2_0 = sb("x2_0", [128, 512], F32)
    x2_1 = sb("x2_1", [128, 512], F32)
    qr_sb = sb("qr_sb", [128, H * 4 * 128], BF16)
    kr_sb = sb("kr_sb", [128, G * 5 * 128], BF16)
    vr_sb = sb("vr_sb", [128, 4 * 256], BF16)
    xqf_sb = sb("xqf_sb", [128, NC_ * 256], F32)

    pf = [stack.enter_context(nc.psum_tensor(f"pf{i}", [128, 512], F32))
          for i in range(6)]
    pb = [stack.enter_context(nc.psum_tensor(f"pb{i}", [128, 512], BF16))
          for i in range(2)]
    pfa = [p[:] for p in pf]
    pba = [p[:] for p in pb]

    sems = {}
    for k in ("pe", "v", "sc", "w", "x", "st", "cw", "gp"):
        sems[k] = stack.enter_context(nc.semaphore(f"s_{k}"))
    g = Gen(sems)

    silu_b = lambda j: (silu0 if j == 0 else silu1)[:]
    x2_b = lambda j: (x2_0 if j == 0 else x2_1)[:]
    def quarter(j):
        buf = pwA if j < 4 else (pwB if j < 8 else pwC)
        return buf[:, (j % 4) * 2048:(j % 4 + 1) * 2048]

    aps = dict(ones_bf=ones_bf[:], ones_bf_ready=None)

    q_last_read = [0] * 10
    q_cw = [0] * 10
    CACHE_RT = {"wq": 99, "wo": 99, "wgu": 99, "wd": 99}
    cache_wr = {k: {} for k in CACHE_RT}
    CONV_RT = {"wq": 0, "wo": 1, "wgu": 99, "wd": 99}
    conv_done = {k: {} for k in CONV_RT}

    def gq_dma(out, in_, pe_war=0):
        g.wait("gpsimd", "pe", pe_war)
        return g.op("gpsimd", lambda e, o=out, i=in_: e.dma_start(out=o, in_=i),
                    inc="w", n=16)

    def slab_load(stream, idx, rt, dst, fp32_slab, bf16_slab, quarters):
        crt = CACHE_RT[stream]
        g.wait("gpsimd", "pe", max(q_last_read[j] for j in quarters))
        g.wait("gpsimd", "cw", max(q_cw[j] for j in quarters))
        if conv_done.get(stream, {}).get(idx) is not None and rt > CONV_RT[stream]:
            g.wait("gpsimd", "w", conv_done[stream][idx])
            val = g.op("gpsimd", lambda e, o=dst, i=bf16_slab:
                       e.dma_start(out=o, in_=i), inc="w", n=16)
        elif rt <= crt:
            val = g.op("gpsimd", lambda e, o=dst, i=fp32_slab:
                       e.dma_start(out=o, in_=i), inc="w", n=16)
        else:
            g.wait("gpsimd", "cw", cache_wr[stream][idx])
            val = g.op("gpsimd", lambda e, o=dst, i=bf16_slab:
                       e.dma_start(out=o, in_=i), inc="w", n=16)
        if rt == crt:
            g.wait("scalar", "w", val)
            cwv = g.op("scalar", lambda e, o=bf16_slab, i=dst:
                       e.dma_start(out=o, in_=i), inc="cw", n=16)
            cache_wr[stream][idx] = cwv
            for j in quarters:
                q_cw[j] = cwv
        return val

    def sq_dma(out, in_):
        return g.op("sync", lambda e, o=out, i=in_: e.dma_start(out=o, in_=i),
                    inc="x", n=16)

    x_bq = sq_dma(bq_sb[:], d_bq)
    x_bk = sq_dma(bk_sb[:], d_bk)
    x_bo = sq_dma(bo_sb[:], d_bo)
    w_onesb = gq_dma(ones_bf[:], d_ones)
    w_eye = gq_dma(eye_bf[:], d_eye)
    w_bv = gq_dma(bv_bf[0:1, :], d_bv)
    aps["ones_bf_ready"] = ("w", w_onesb)
    g.wait("vector", "x", x_bq)
    v_bqs = g.op("vector", lambda e: e.tensor_scalar(
        out=bqs_sb[:], in0=bq_sb[:], scalar1=SCL, scalar2=None, op0=ALU.mult),
        inc="v")

    prev_down_mm_stop = 0
    prev_v_stop = 0
    prev_devac = 0
    store_cnt = 0

    def new_chunks(rt):
        return [0, 1, 2] if rt == 0 else [2 * rt + 1, 2 * rt + 2]

    def emit_xqbf_load(rt, pe_gate=0):
        ms = new_chunks(rt)
        c0, T = 128 * ms[0], 128 * len(ms)
        in3 = d_xq[:, c0:c0 + T].rearrange("(c p) t -> p c t", p=128)
        out3b = xq_bf[:].rearrange("p (c t) -> p c t", t=TQ)[:, :, 0:T]
        g.wait("gpsimd", "pe", pe_gate)
        return g.op("gpsimd", lambda e, o=out3b, i=in3:
                    e.dma_start(out=o, in_=i), inc="w", n=16)

    def ln1_passes(rt):
        if rt == 0:
            return [([0, 1], 0, 0), ([2], 256, 128)]
        return [(new_chunks(rt), 0, 0)]

    def emit_xqf_load(pms, v_gate=0):
        ms, off, sh = pms
        c0, T = 128 * ms[0], 128 * len(ms)
        in3 = d_xq[:, c0:c0 + T].rearrange("(c p) t -> p c t", p=128)
        out3 = xqf_sb[:].rearrange("p (c t) -> p c t", t=256)[:, :, sh:sh + T]
        g.wait("gpsimd", "v", v_gate)
        return g.op("gpsimd", lambda e, o=out3, i=in3:
                    e.dma_start(out=o, in_=i), inc="w", n=16)

    def emit_ln1(pms, w_xqbf, w_xqf, down_war):
        ms, off, sh = pms
        T = 128 * len(ms)
        srcf = lambda cc, war: (xqf_sb[:, cc * 256 + sh: cc * 256 + sh + T],
                                "w", w_xqf)
        blk_bf = lambda cc: xq_bf[:, cc * TQ + off: cc * TQ + off + T]
        sq_blk1 = lambda j: sq_sb[:, j * 512:j * 512 + T]
        g.wait("scalar", "pe", down_war)
        mul1, _ = _emit_ln(
            g, aps, NC_, T, srcf, srcf, blk_bf, blk_bf, sq_blk1,
            ("pf0", pfa[0]), ("pf1", pfa[1]), ("pf2", pfa[2]), ("pf3", pfa[3]),
            src_bf_ready=lambda cc: ("w", w_xqbf))
        return mul1

    q_ready, k_ready, v_ready = {}, {}, {}

    pre_xqbf = emit_xqbf_load(0)
    pre_xqf = [emit_xqf_load(ln1_passes(0)[0])]

    for rt in range(RT):
        ascope = contextlib.ExitStack()
        asb = lambda name, shape, dt: ascope.enter_context(
            nc.sbuf_tensor(f"{name}_{rt}", shape, dt))
        attn_sb = asb("attn", [128, NC_ * TO], BF16)
        wv_sb = asb("wv_sb", [128, 4096], BF16)
        mean_row = asb("mean_row", [1, 512], F32)
        msq_row = asb("msq_row", [1, 512], F32)
        tmp_row = asb("tmp_row", [1, 512], F32)
        mean_bf = asb("mean_bf", [1, 512], BF16)
        rstd_bf = asb("rstd_bf", [1, 512], BF16)
        tmp_ln0 = asb("tmp_ln0", [128, 512], F32)
        tmp_ln1 = asb("tmp_ln1", [128, 512], F32)
        sq_sb = asb("sq_sb", [128, 1024], BF16)
        nm_sb = asb("nm_sb", [128, 8], F32)
        rs_sb = asb("rs_sb", [128, 8], F32)
        ri_sb = asb("ri_sb", [128, 8], F32)
        tmp_ln = lambda j: (tmp_ln0 if j == 0 else tmp_ln1)[:]
        aps.update(mean_row=mean_row[:], msq_row=msq_row[:], tmp_row=tmp_row[:],
                   mean_bf=mean_bf[:], rstd_bf=rstd_bf[:], tmp_ln=tmp_ln)
        p_sb0 = asb("p0", [128, 512], BF16)
        p_sb1 = asb("p1", [128, 512], BF16)
        p_sb2 = asb("p2", [128, 512], BF16)
        p_sb3 = asb("p3", [128, 512], BF16)
        pt_sb0 = asb("pt0", [128, 512], BF16)
        pt_sb1 = asb("pt1", [128, 512], BF16)
        p_sbs = [p_sb0, p_sb1, p_sb2, p_sb3]
        p_sb = lambda j: p_sbs[j][:]
        pt_sb = lambda j: (pt_sb0 if j == 0 else pt_sb1)[:]

        ms = new_chunks(rt)
        Tn = 128 * len(ms)

        in3o = d_xo[:, rt * TO:(rt + 1) * TO].rearrange("(c p) t -> p c t", p=128)
        out3o = xo_sb[:].rearrange("p (c t) -> p c t", t=TO)
        g.wait("sync", "v", prev_devac)
        x_xo = sq_dma(out3o, in3o)
        wq_pre = {}
        for fc in range(2):
            wq_pre[fc] = slab_load("wq", fc, rt, quarter(fc % 4), d_wq[fc],
                                   s_wq[fc], [fc % 4])
        wk_q = {}
        for gix in range(2):
            g.wait("gpsimd", "pe", q_last_read[4 + gix])
            wk_q[gix] = g.op("gpsimd", lambda e, o=quarter(4 + gix), i=d_wk[gix]:
                             e.dma_start(out=o, in_=i), inc="w", n=16)
        w_wv = gq_dma(wv_sb[:], d_wv, pe_war=prev_down_mm_stop)

        for pi, pms in enumerate(ln1_passes(rt)):
            if pi >= len(pre_xqf):
                pre_xqf.append(emit_xqf_load(pms, v_gate=mul1[NC_ - 1]))
            mul1 = emit_ln1(pms, pre_xqbf, pre_xqf[pi],
                            down_war=prev_down_mm_stop)

        blkn_bf = lambda hc: xq_bf[:, hc * TQ: hc * TQ + Tn]
        for fc in range(NC_):
            qd = fc % 4
            if fc in wq_pre:
                w_slab = wq_pre[fc]
            else:
                w_slab = slab_load("wq", fc, rt, quarter(qd), d_wq[fc], s_wq[fc], [qd])
            bank = f"pf{fc % 2}"
            psq = pfa[fc % 2][:, 0:Tn]
            g.use_bank(bank)
            g.wait("tensor", "w", w_slab)
            for hc in range(NC_):
                g.wait("tensor", "v", mul1[hc])
                g.op("tensor", lambda e, o=psq, l=quarter(qd)[:, hc * 128:(hc + 1) * 128],
                     r=blkn_bf(hc), s=(hc == 0), t=(hc == NC_ - 1):
                     e.matmul(o, l, r, start=s, stop=t),
                     inc="pe" if hc == NC_ - 1 else None)
            mm_stop = g.cnt["pe"]
            q_last_read[qd] = mm_stop
            g.wait("scalar", "pe", mm_stop)
            g.wait("scalar", "v", v_bqs)
            ev = None
            for j, m in enumerate(ms):
                ev = g.op("scalar", lambda e,
                          o=qr_sb[:, fc * 512 + (m % 4) * 128: fc * 512 + (m % 4) * 128 + 128],
                          i=pfa[fc % 2][:, j * 128:(j + 1) * 128],
                          bias=bqs_sb[:, fc:fc + 1]:
                          e.activation(out=o, in_=i, func=AF.Identity,
                                       bias=bias, scale=SCL),
                          inc="sc" if j == len(ms) - 1 else None)
            g.consume_bank(bank, "sc", ev)
        for m in ms:
            q_ready[m] = g.cnt["sc"]
        if rt == CONV_RT["wq"]:
            for fc in range(NC_):
                conv_done["wq"][fc] = g.op(
                    "gpsimd", lambda e, o=s_wq[fc], i=d_wq[fc]:
                    e.dma_start(out=o, in_=i), inc="w", n=16)
        if rt == CONV_RT["wo"]:
            for fc in range(NC_):
                conv_done["wo"][fc] = g.op(
                    "gpsimd", lambda e, o=s_wo[fc], i=d_wo[fc]:
                    e.dma_start(out=o, in_=i), inc="w", n=16)
        for gix in range(2):
            bank = f"pf{gix % 2}"
            psk = pfa[gix % 2][:, 0:Tn]
            g.use_bank(bank)
            g.wait("tensor", "w", wk_q[gix])
            for hc in range(NC_):
                g.op("tensor", lambda e, o=psk,
                     l=quarter(4 + gix)[:, hc * 128:(hc + 1) * 128],
                     r=blkn_bf(hc), s=(hc == 0), t=(hc == NC_ - 1):
                     e.matmul(o, l, r, start=s, stop=t),
                     inc="pe" if hc == NC_ - 1 else None)
            mm_stop = g.cnt["pe"]
            q_last_read[4 + gix] = mm_stop
            g.wait("scalar", "pe", mm_stop)
            g.wait("scalar", "x", x_bk)
            ev = None
            for j, m in enumerate(ms):
                slots = [m % 4] + ([4] if m % 4 == 0 else [])
                for s4 in slots:
                    ev = g.op("scalar", lambda e,
                              o=kr_sb[:, gix * 640 + s4 * 128: gix * 640 + s4 * 128 + 128],
                              i=pfa[gix % 2][:, j * 128:(j + 1) * 128],
                              bias=bk_sb[:, gix:gix + 1]:
                              e.activation(out=o, in_=i, func=AF.Identity,
                                           bias=bias),
                              inc="sc" if (j == len(ms) - 1 and s4 == slots[-1]) else None)
            g.consume_bank(bank, "sc", ev)
        for m in ms:
            k_ready[m] = g.cnt["sc"]
        g.wait("tensor", "w", w_wv)
        g.wait("tensor", "w", w_bv)
        for j in range(len(ms)):
            g.use_bank(f"pf{2 + j}")
        for hc in range(NC_):
            for j in range(len(ms)):
                psv = pfa[2 + j][:, 0:256]
                g.op("tensor", lambda e, o=psv,
                     l=xq_bf[:, hc * TQ + j * 128: hc * TQ + (j + 1) * 128],
                     r=wv_sb[:, hc * 256:(hc + 1) * 256], s=(hc == 0), t=False:
                     e.matmul(o, l, r, start=s, stop=t))
        v_stop = [None] * len(ms)
        for j in range(len(ms)):
            psv = pfa[2 + j][:, 0:256]
            v_stop[j] = g.op("tensor", lambda e, o=psv, l=ones_bf[0:1, 0:128],
                             r=bv_bf[0:1, 0:256]:
                             e.matmul(o, l, r, start=False, stop=True), inc="pe")
        prev_v_stop = g.cnt["pe"]
        for j, m in enumerate(ms):
            g.wait("vector", "pe", v_stop[j])
            v_ready[m] = g.op("vector", lambda e,
                              o=vr_sb[:, (m % 4) * 256:(m % 4) * 256 + 256],
                              i=pfa[2 + j][:, 0:256]: e.tensor_copy(out=o, in_=i),
                              inc="v")
            g.consume_bank(f"pf{2 + j}", "v", v_ready[m])

        SBANKS = [0, 1, 4, 5]
        exp_done = [None] * 32
        scale_done = [None] * 32
        ptev_done = [None] * 32
        oev_done = [None] * 32
        s_stop = [None] * 32
        pt_stop = [None] * 32
        o_stop = [None] * 32

        def at_S(wh):
            wl, h = wh // 16, wh % 16
            w = 2 * rt + wl
            gg = h % 2
            bi = SBANKS[wh % 4]
            g.use_bank(f"pf{bi}")
            g.wait("tensor", "sc", q_ready[w + 1])
            g.wait("tensor", "sc", k_ready[w + 1])
            for qc in range(2):
                sl = (w + qc) % 4
                s_stop[wh] = g.op(
                    "tensor", lambda e, o=pfa[bi][:, qc * 256:(qc + 1) * 256],
                    l=qr_sb[:, h * 512 + sl * 128: h * 512 + sl * 128 + 128],
                    r=kr_sb[:, gg * 640 + (w % 4) * 128: gg * 640 + (w % 4) * 128 + 256]:
                    e.matmul(o, l, r, start=True, stop=True),
                    inc="pe" if qc == 1 else None)

        def at_max(wh):
            bi = SBANKS[wh % 4]
            g.wait("vector", "pe", s_stop[wh])
            if wh >= 4:
                g.wait("vector", "sc", exp_done[wh - 4])
            sp3 = pfa[bi].rearrange("p (a k) -> p a k", k=256)
            g.op("vector", lambda e, o=nm_sb[:, (wh % 4) * 2: (wh % 4) * 2 + 2],
                 i=sp3: e.reduce_max(out=o, in_=i, axis=AX.X, negate=True),
                 inc="v")

        nm_done = [None] * 32

        def at_exp(wh):
            bi = SBANKS[wh % 4]
            g.wait("scalar", "v", nm_done[wh])
            if wh >= 4:
                g.wait("scalar", "pe", pt_stop[wh - 4])
            for qc in range(2):
                exp_done[wh] = g.op(
                    "scalar", lambda e, o=p_sb(wh % 4)[:, qc * 256:(qc + 1) * 256],
                    i=pfa[bi][:, qc * 256:(qc + 1) * 256],
                    b=nm_sb[:, (wh % 4) * 2 + qc: (wh % 4) * 2 + qc + 1]:
                    e.activation(out=o, in_=i, func=AF.Exp, bias=b),
                    inc="sc" if qc == 1 else None)
            g.consume_bank(f"pf{bi}", "sc", exp_done[wh])

        def at_norm(wh):
            g.wait("vector", "sc", exp_done[wh])
            p3 = p_sb(wh % 4).rearrange("p (a k) -> p a k", k=256)
            sum_done = g.op("vector",
                            lambda e, o=rs_sb[:, (wh % 4) * 2: (wh % 4) * 2 + 2],
                            i=p3: e.reduce_sum(out=o, in_=i, axis=AX.X), inc="v")
            g.wait("vector", "v", sum_done)
            ri_done = g.op("vector",
                           lambda e, o=ri_sb[:, (wh % 4) * 2: (wh % 4) * 2 + 2],
                           i=rs_sb[:, (wh % 4) * 2: (wh % 4) * 2 + 2]:
                           e.reciprocal(out=o, in_=i), inc="v")
            g.wait("gpsimd", "v", ri_done)
            for qc in range(2):
                scale_done[wh] = g.op(
                    "gpsimd", lambda e, o=p_sb(wh % 4)[:, qc * 256:(qc + 1) * 256],
                    s1=ri_sb[:, (wh % 4) * 2 + qc: (wh % 4) * 2 + qc + 1]:
                    e.tensor_scalar(out=o, in0=o, scalar1=s1, scalar2=1.0,
                                    op0=ALU.mult, op1=ALU.mult),
                    inc="gp" if qc == 1 else None)

        def at_PT(wh):
            g.use_bank(f"pb{wh % 2}")
            g.wait("tensor", "gp", scale_done[wh])
            for qc in range(2):
                for kc in range(2):
                    pt_stop[wh] = g.op(
                        "tensor", lambda e,
                        o=pba[wh % 2][:, kc * 256 + qc * 128: kc * 256 + (qc + 1) * 128],
                        i=p_sb(wh % 4)[:, qc * 256 + kc * 128: qc * 256 + (kc + 1) * 128]:
                        e.transpose(o, i, eye_bf[:]),
                        inc="pe" if (qc == 1 and kc == 1) else None)

        def at_ptevac(wh):
            g.wait("scalar", "pe", pt_stop[wh])
            if wh >= 2:
                g.wait("scalar", "pe", o_stop[wh - 2])
            ptev_done[wh] = g.op("scalar", lambda e, o=pt_sb(wh % 2), i=pba[wh % 2]:
                                 e.activation(out=o, in_=i, func=AF.Identity),
                                 inc="sc")
            g.consume_bank(f"pb{wh % 2}", "sc", ptev_done[wh])

        def at_O(wh):
            wl, gg = wh // 16, (wh % 16) % 2
            w = 2 * rt + wl
            g.use_bank(f"pf{2 + wh % 2}")
            g.wait("tensor", "sc", ptev_done[wh])
            g.wait("tensor", "v", v_ready[w + 1])
            for kc in range(2):
                sl = (w + kc) % 4
                o_stop[wh] = g.op(
                    "tensor", lambda e, o=pfa[2 + wh % 2][:, 0:256],
                    l=vr_sb[:, sl * 256 + gg * 128: sl * 256 + (gg + 1) * 128],
                    r=pt_sb(wh % 2)[:, kc * 256:(kc + 1) * 256], s=(kc == 0), t=(kc == 1):
                    e.matmul(o, l, r, start=s, stop=t),
                    inc="pe" if kc == 1 else None)

        def at_oevac(wh):
            wl, h = wh // 16, wh % 16
            g.wait("vector", "pe", o_stop[wh])
            oev_done[wh] = g.op(
                "vector", lambda e, o=attn_sb[:, h * TO + wl * 256: h * TO + wl * 256 + 256],
                i=pfa[2 + wh % 2][:, 0:256]: e.tensor_copy(out=o, in_=i), inc="v")
            g.consume_bank(f"pf{2 + wh % 2}", "v", oev_done[wh])

        for i in range(32 + 8):
            if i < 32:
                at_S(i)
            if 0 <= i - 1 < 32:
                at_max(i - 1)
                nm_done[i - 1] = g.cnt["v"]
            if 0 <= i - 2 < 32:
                at_exp(i - 2)
            if 0 <= i - 3 < 32:
                at_norm(i - 3)
            if 0 <= i - 4 < 32:
                at_PT(i - 4)
            if 0 <= i - 5 < 32:
                at_ptevac(i - 5)
            if 0 <= i - 6 < 32:
                at_O(i - 6)
            if 0 <= i - 7 < 32:
                at_oevac(i - 7)
        attn_ready = oev_done[31]

        stt_done = [None] * NC_
        copy_done = [None] * NC_
        for fc in range(NC_):
            qd = fc % 4
            w_slab = slab_load("wo", fc, rt, quarter(qd), d_wo[fc], s_wo[fc], [qd])
            bank = f"pf{fc % 2}"
            psx = pfa[fc % 2][:, 0:TO]
            g.use_bank(bank)
            g.wait("tensor", "w", w_slab)
            g.wait("tensor", "v", attn_ready)
            for hc in range(NC_):
                g.op("tensor", lambda e, o=psx,
                     l=quarter(qd)[:, hc * 128:(hc + 1) * 128],
                     r=attn_sb[:, hc * TO:(hc + 1) * TO], s=(hc == 0),
                     t=(hc == NC_ - 1): e.matmul(o, l, r, start=s, stop=t),
                     inc="pe" if hc == NC_ - 1 else None)
            mm_stop = g.cnt["pe"]
            q_last_read[qd] = mm_stop
            g.wait("vector", "pe", mm_stop)
            g.wait("vector", "x", x_xo)
            g.wait("vector", "x", x_bo)
            stt_done[fc] = g.op(
                "vector", lambda e, o=xo_sb[:, fc * TO:(fc + 1) * TO], i=psx,
                s1=bo_sb[:, fc:fc + 1]:
                e.scalar_tensor_tensor(out=o, in0=i, scalar=s1, in1=o,
                                       op0=ALU.add, op1=ALU.add), inc="v")
            g.consume_bank(bank, "v", stt_done[fc])
            if fc >= 1:
                g.wait("vector", "v", stt_done[fc - 1])
                copy_done[fc - 1] = g.op(
                    "vector", lambda e, o=ln2_bf[:, (fc - 1) * TO: fc * TO],
                    i=xo_sb[:, (fc - 1) * TO: fc * TO]:
                    e.tensor_copy(out=o, in_=i), inc="v")
        g.wait("vector", "v", stt_done[NC_ - 1])
        copy_done[NC_ - 1] = g.op(
            "vector", lambda e, o=ln2_bf[:, (NC_ - 1) * TO: NC_ * TO],
            i=xo_sb[:, (NC_ - 1) * TO: NC_ * TO]: e.tensor_copy(out=o, in_=i),
            inc="v")

        blko_f = lambda cc: xo_sb[:, cc * TO:(cc + 1) * TO]
        blk2_bf = lambda cc: ln2_bf[:, cc * TO:(cc + 1) * TO]
        sq_blk2 = lambda j: sq_sb[:, j * 512:(j + 1) * 512]
        src2 = lambda cc, war: (blko_f(cc), "v", stt_done[cc])
        mul2, _ = _emit_ln(
            g, aps, NC_, TO, src2, src2, blk2_bf, blk2_bf, sq_blk2,
            ("pf4", pfa[4]), ("pf5", pfa[5]), ("pf2", pfa[2]), ("pf3", pfa[3]),
            src_bf_ready=lambda cc: ("v", copy_done[cc]))

        ascope.close()
        fscope = contextlib.ExitStack()
        act_sb = fscope.enter_context(
            nc.sbuf_tensor(f"act_{rt}", [128, NI * TO], BF16))

        silu_done = [None] * NI
        act_done = [None] * NI
        w_wd0 = None
        for ic in range(NI):
            if ic == 2:
                w_wd0 = slab_load("wd", 0, rt, pwA[:], d_wd[0], s_wd[0],
                                  [0, 1, 2, 3])
            qd_g = 4 + (2 * ic) % 5
            w_g = slab_load("wgu", ic, rt, quarter(qd_g), d_wgu[ic],
                            s_wgu[ic], [qd_g])
            qd_u = 4 + (2 * ic + 1) % 5
            w_u = slab_load("wgu", 64 + ic, rt, quarter(qd_u), d_wgu[64 + ic],
                            s_wgu[64 + ic], [qd_u])
            gb = [0, 1, 4][ic % 3]
            ub = [2, 3, 5][ic % 3]
            gbank, ubank = f"pf{gb}", f"pf{ub}"
            psg = pfa[gb][:, 0:TO]
            psu = pfa[ub][:, 0:TO]
            g.use_bank(gbank)
            g.wait("tensor", "w", w_g)
            for hc in range(NC_):
                g.wait("tensor", "v", mul2[hc])
                g.op("tensor", lambda e, o=psg,
                     l=quarter(qd_g)[:, hc * 128:(hc + 1) * 128],
                     r=blk2_bf(hc), s=(hc == 0), t=(hc == NC_ - 1):
                     e.matmul(o, l, r, start=s, stop=t),
                     inc="pe" if hc == NC_ - 1 else None)
            g_stop = g.cnt["pe"]
            q_last_read[qd_g] = g_stop
            g.use_bank(ubank)
            g.wait("tensor", "w", w_u)
            for hc in range(NC_):
                g.op("tensor", lambda e, o=psu,
                     l=quarter(qd_u)[:, hc * 128:(hc + 1) * 128],
                     r=blk2_bf(hc), s=(hc == 0), t=(hc == NC_ - 1):
                     e.matmul(o, l, r, start=s, stop=t),
                     inc="pe" if hc == NC_ - 1 else None)
            u_stop = g.cnt["pe"]
            q_last_read[qd_u] = u_stop
            g.wait("scalar", "pe", g_stop)
            if ic >= 2:
                g.wait("scalar", "v", act_done[ic - 2])
            silu_done[ic] = g.op("scalar", lambda e, o=silu_b(ic % 2), i=psg:
                                 e.activation(out=o, in_=i, func=AF.Silu),
                                 inc="sc")
            g.consume_bank(gbank, "sc", silu_done[ic])
            g.wait("vector", "sc", silu_done[ic])
            g.wait("vector", "pe", u_stop)
            act_done[ic] = g.op(
                "vector", lambda e, o=act_sb[:, ic * TO:(ic + 1) * TO],
                a=silu_b(ic % 2), b=psu:
                e.tensor_tensor(out=o, in0=a, in1=b, op=ALU.mult), inc="v")
            g.consume_bank(ubank, "v", act_done[ic])

        if rt + 1 < RT:
            pre_xqbf = emit_xqbf_load(rt + 1, pe_gate=prev_v_stop)
            pre_xqf = [emit_xqf_load(p, v_gate=mul1[NC_ - 1])
                       for p in ln1_passes(rt + 1)]

        for cc in range(NC_):
            if cc % 2 == 0:
                wd_buf = pwA
                quarters = [0, 1, 2, 3]
            else:
                wd_buf = pwB
                quarters = [4, 5, 6, 7]
            if cc == 0:
                w_wd = w_wd0
            else:
                w_wd = slab_load("wd", cc, rt, wd_buf[:], d_wd[cc], s_wd[cc],
                                 quarters)
            bank = f"pf{4 + cc % 2}"
            psd = pfa[4 + cc % 2][:, 0:TO]
            g.use_bank(bank)
            g.wait("tensor", "w", w_wd)
            for ic in range(NI):
                g.wait("tensor", "v", act_done[ic])
                g.op("tensor", lambda e, o=psd,
                     l=wd_buf[:, ic * 128:(ic + 1) * 128],
                     r=act_sb[:, ic * TO:(ic + 1) * TO], s=(ic == 0),
                     t=(ic == NI - 1): e.matmul(o, l, r, start=s, stop=t),
                     inc="pe" if ic == NI - 1 else None)
            mm_stop = g.cnt["pe"]
            if cc % 2 == 0:
                for j in range(4):
                    q_last_read[j] = mm_stop
            else:
                for j in range(4, 8):
                    q_last_read[j] = mm_stop
            g.wait("vector", "pe", mm_stop)
            g.wait("vector", "st", 16 * max(0, store_cnt - 1))
            devac = g.op("vector", lambda e, o=x2_b(cc % 2), i=psd,
                         b=xo_sb[:, cc * TO:(cc + 1) * TO]:
                         e.tensor_tensor(out=o, in0=i, in1=b, op=ALU.add),
                         inc="v")
            g.consume_bank(bank, "v", devac)
            g.wait("sync", "v", devac)
            g.op("sync", lambda e, o=d_y[cc * 128:(cc + 1) * 128,
                                         rt * TO:(rt + 1) * TO], i=x2_b(cc % 2):
                 e.dma_start(out=o, in_=i), inc="st", n=16)
            store_cnt += 1
            prev_devac = devac
        prev_down_mm_stop = g.cnt["pe"]
        fscope.close()

    g.op("sync", lambda e, v=16 * store_cnt: e.wait_ge(sems["st"], v))

    with nc.Block() as block:
        @block.sync
        def _(sync):
            for f in g.prog["sync"]:
                f(sync)

        @block.gpsimd
        def _(gpsimd):
            for f in g.prog["gpsimd"]:
                f(gpsimd)

        @block.tensor
        def _(tensor):
            for f in g.prog["tensor"]:
                f(tensor)

        @block.vector
        def _(vector):
            for f in g.prog["vector"]:
                f(vector)

        @block.scalar
        def _(scalar):
            for f in g.prog["scalar"]:
                f(scalar)

    stack.close()
    return nc


_cache = {}


def kernel(L0_attn_x, L0_attn_Wq, L0_attn_bq, L0_attn_Wk, L0_attn_bk,
           L0_attn_Wv, L0_attn_bv, L0_attn_Wo, L0_attn_bo,
           L0_ffn_w_gate_up, L0_ffn_w_down):
    x = np.asarray(L0_attn_x, dtype=np.float32)
    Wq = np.asarray(L0_attn_Wq, dtype=np.float32)
    Wk = np.asarray(L0_attn_Wk, dtype=np.float32)
    Wv = np.asarray(L0_attn_Wv, dtype=np.float32)
    Wo = np.asarray(L0_attn_Wo, dtype=np.float32)
    Wgu = np.asarray(L0_ffn_w_gate_up, dtype=np.float32)
    Wd = np.asarray(L0_ffn_w_down, dtype=np.float32)

    wq_blk = np.ascontiguousarray(Wq.reshape(16, 128, 16, 128).transpose(2, 1, 0, 3))
    wk_blk = np.ascontiguousarray(Wk.reshape(16, 128, 2, 128).transpose(2, 1, 0, 3))
    wo_blk = np.ascontiguousarray(Wo.reshape(16, 128, 16, 128).transpose(2, 1, 0, 3))
    wv_blk = np.ascontiguousarray(
        Wv.reshape(16, 128, 256).transpose(1, 0, 2)).reshape(128, 4096)
    wgu_blk = np.ascontiguousarray(
        Wgu.reshape(128, 128, 16, 128).transpose(0, 3, 2, 1))
    wd_blk = np.ascontiguousarray(
        Wd.reshape(16, 128, 64, 128).transpose(0, 3, 2, 1))
    bq2 = np.ascontiguousarray(np.asarray(L0_attn_bq, np.float32).reshape(16, 128).T)
    bk2 = np.ascontiguousarray(np.asarray(L0_attn_bk, np.float32).reshape(2, 128).T)
    bo2 = np.ascontiguousarray(np.asarray(L0_attn_bo, np.float32).reshape(16, 128).T)
    bv2 = np.ascontiguousarray(np.asarray(L0_attn_bv, np.float32).reshape(1, 256))
    ones = np.ones((128, 128), np.float32)
    eye = np.eye(128, dtype=np.float32)

    if "nc" not in _cache:
        _cache["nc"] = build_nc()
    nc = _cache["nc"]

    in_maps = []
    for c in range(NCORE):
        b, a = c // 4, c % 4
        xqT = np.ascontiguousarray(x[b, 1024 * a:1024 * a + 1152, :].T)
        xoT = np.ascontiguousarray(x[b, 2048 * a:2048 * (a + 1), :].T)
        in_maps.append(dict(
            xq=xqT, xo=xoT, wq=wq_blk, wk=wk_blk, wv=wv_blk, wo=wo_blk,
            wgu=wgu_blk, wd=wd_blk, bq=bq2, bk=bk2, bo=bo2, bv=bv2,
            ones=ones, eye=eye))

    res = run_bass_kernel_spmd(nc, in_maps, core_ids=list(range(NCORE)))

    out = np.empty((B, L, C), np.float32)
    for c in range(NCORE):
        b, a = c // 4, c % 4
        out[b, 2048 * a:2048 * (a + 1), :] = np.asarray(res.results[c]["y"]).T
    return out
